# version 21
# baseline (speedup 1.0000x reference)
import numpy as np
from contextlib import ExitStack

import concourse.bass as bass
import concourse.mybir as mybir
from concourse.bass_utils import run_bass_kernel_spmd

F32 = mybir.dt.float32
BF16 = mybir.dt.bfloat16
I32 = mybir.dt.int32
AF = mybir.ActivationFunctionType
OP = mybir.AluOpType

D = 1024
T = 4096
NCH = 8
HD = 64
F = 4096
FCH = 32
TT = 256
NT2 = T // TT
P1T = 512
NT1 = T // P1T
NSUB = P1T // 128
EPS_LN = 1e-5
EPS_ATTN = 1e-6

W1_DEFER = ()
W2_DEFER = tuple(range(16, 32))


class Planner:
    ENGINES = ("tensor", "vector", "scalar", "sync", "gpsimd")

    def __init__(self):
        self.ev = {e: [] for e in self.ENGINES}
        self.cnt = {"T": 0, "V": 0, "S": 0, "Xx": 0, "Xw": 0, "Xq": 0,
                    "Xd0": 0, "Xd1": 0, "Xd2": 0, "Xs0": 0, "Xs1": 0, "ST": 0}

    def do(self, eng, fn):
        self.ev[eng].append(("do", fn))

    def wait(self, eng, key, val):
        if val > 0:
            self.ev[eng].append(("wait", key, val))

    def inc(self, key, n=1):
        self.cnt[key] += n
        return self.cnt[key]


def build_nc(debug=False, p1_only=False, max_iters=None, prol=4, qsub=5):
    nc = bass.Bass("TRN2", target_bir_lowering=False)

    xt = nc.dram_tensor("xt", [D, T], F32, kind="ExternalInput").ap()
    wq_t = nc.dram_tensor("wq_t", [D, D], F32, kind="ExternalInput").ap()
    wk_t = nc.dram_tensor("wk_t", [D, D], F32, kind="ExternalInput").ap()
    wv_t = nc.dram_tensor("wv_t", [D, D], F32, kind="ExternalInput").ap()
    wo_t = nc.dram_tensor("wo_t", [D, D], F32, kind="ExternalInput").ap()
    bq_d = nc.dram_tensor("bq_d", [128, NCH], F32, kind="ExternalInput").ap()
    bo_d = nc.dram_tensor("bo_d", [128, NCH], F32, kind="ExternalInput").ap()
    bias2_d = nc.dram_tensor("bias2_d", [2, D], F32, kind="ExternalInput").ap()
    sel2_d = nc.dram_tensor("sel2_d", [2, 256], F32, kind="ExternalInput").ap()
    gam_d = nc.dram_tensor("gam_d", [128, NCH], F32, kind="ExternalInput").ap()
    bet_d = nc.dram_tensor("bet_d", [128, NCH], F32, kind="ExternalInput").ap()
    fc1b_d = nc.dram_tensor("fc1b_d", [128, FCH], F32, kind="ExternalInput").ap()
    fc2b_d = nc.dram_tensor("fc2b_d", [128, NCH], F32, kind="ExternalInput").ap()
    q1_d = nc.dram_tensor("q1_d", [D, F], I32, kind="ExternalInput").ap()
    q2_d = nc.dram_tensor("q2_d", [F, D], I32, kind="ExternalInput").ap()
    sc1_d = nc.dram_tensor("sc1_d", [1, NCH * F], F32, kind="ExternalInput").ap()
    sc2_d = nc.dram_tensor("sc2_d", [1, FCH * D], F32, kind="ExternalInput").ap()
    sel_d = nc.dram_tensor("sel_d", [16, D], F32, kind="ExternalInput").ap()
    onec_d = nc.dram_tensor("onec_d", [128, 1], F32, kind="ExternalInput").ap()
    oner_d = nc.dram_tensor("oner_d", [1, 128], F32, kind="ExternalInput").ap()
    out_t = nc.dram_tensor("out_t", [D, T], F32, kind="ExternalOutput").ap()

    dbg = {}
    if debug:
        for name, shape, dt in [("kv", [128, NCH * HD], BF16), ("ksbd", [128, 128], BF16),
                                ("q2", [128, 2048], BF16), ("nrow", [16, TT], BF16),
                                ("q3", [128, 2048], BF16), ("mg", [128, 2048], BF16),
                                ("r1", [128, 2048], BF16), ("x2", [128, 2048], BF16),
                                ("h", [128, 2048], BF16), ("y0", [128, 4096], BF16),
                                ("w", [128, 2048], BF16), ("ktm0", [128, 1024], BF16),
                                ("vtm0", [128, 1040], BF16)]:
            dbg[name] = nc.dram_tensor("dbg_" + name, shape, dt, kind="ExternalOutput").ap()

    with ExitStack() as stack:
        E = stack.enter_context
        wq_bf = E(nc.sbuf_tensor("wq_bf", [128, NCH * D], BF16))
        wo_bf = E(nc.sbuf_tensor("wo_bf", [128, NCH * D], BF16))
        w1_bf = E(nc.sbuf_tensor("w1_bf", [128, NCH * F], BF16))
        w2_bf = E(nc.sbuf_tensor("w2_bf", [128, FCH * D], BF16))
        kv_sb = E(nc.sbuf_tensor("kv_sb", [128, NCH * HD], BF16))
        ksum_bd = E(nc.sbuf_tensor("ksum_bd", [128, NCH * 16], BF16))
        onec_bf = E(nc.sbuf_tensor("onec_bf", [128, 1], BF16))
        bq_pp = E(nc.sbuf_tensor("bq_pp", [128, NCH], F32))
        bq1_pp = E(nc.sbuf_tensor("bq1_pp", [128, NCH], F32))
        bo_pp = E(nc.sbuf_tensor("bo_pp", [128, NCH], F32))
        gam_pp = E(nc.sbuf_tensor("gam_pp", [128, NCH], F32))
        bet_pp = E(nc.sbuf_tensor("bet_pp", [128, NCH], F32))
        fc1b_pp = E(nc.sbuf_tensor("fc1b_pp", [128, FCH], F32))
        fc2bh_pp = E(nc.sbuf_tensor("fc2bh_pp", [128, NCH], F32))

        smalls = E(nc.sbuf_tensor("smalls", [128, 3840], BF16))
        selb = smalls[0:16, 0:1024]
        nrow_bf = smalls[0:16, 1024:1280]
        row_c = smalls[0:1, 1280:1792].bitcast(F32)
        row_b = smalls[0:1, 1792:2304].bitcast(F32)
        oner_f = smalls[0:1, 2304:2560].bitcast(F32)
        bias2 = smalls[0:2, 2560:3584]
        sel2k = smalls[0:2, 3584:3712]
        sel2v = smalls[0:2, 3712:3840]

        ACOLS = 19712
        arena = E(nc.sbuf_tensor("arena", [128, ACOLS], BF16))
        xbf = arena[:, 0:2048]
        qbf = arena[:, 2048:4096]
        esc = [arena[:, 4096:4352], arena[:, 18688:18944]]
        sqb = arena[:, 4352:6400]
        r1bf = arena[:, 6400:8448]
        x2s = [arena[:, 8448:10496], arena[:, 10496:12544]]
        hbf = arena[:, 12544:14592]
        ybf = arena[:, 14592:18688]
        mgbf = qbf
        outfull = arena[:, 14592:18688].bitcast(F32)
        xp1 = [arena[:, 0:4096], arena[:, 4096:8192]]
        int_st = [arena[:, 10496:11008], arena[:, 12032:12544],
                  arena[:, 18944:19456]]
        scbc = [arena[:, 11008:12032].bitcast(F32),
                arena[:, 13568:14592].bitcast(F32)]
        p1esc = [arena[:, 12544 + 512 * i:13056 + 512 * i] for i in range(2)]
        kv_acc = arena[:, 8192:10272].bitcast(F32)
        zacc = arena[:, 6400:8448]
        ktm = [arena[:, 14592:15616], arena[:, 15616:16640]]
        vtm = [arena[:, 16640 + 1040 * i:17680 + 1040 * i] for i in range(2)]
        wk_bf = w2_bf[:, 16 * D:24 * D]
        wv_bf = w2_bf[:, 24 * D:32 * D]

        P = [E(nc.psum_tensor(f"P{i}", [128, 512], F32)) for i in range(8)]
        global LAST_VIEWS
        LAST_VIEWS = {
            "r1bf": r1bf, "sqb": sqb, "row_c": row_c, "row_b": row_b,
            "qbf": qbf, "xbf": xbf, "mgbf": mgbf, "kv_acc": kv_acc,
            "P6": P[6][:], "P7": P[7][:], "kv_sb": kv_sb[:], "ksum_bd": ksum_bd[:],
            "x2a": x2s[0], "hbf": hbf, "esc": esc[0], "nrow_bf": nrow_bf,
            "out_t": out_t, "w1_bf": w1_bf[:], "w2_bf": w2_bf[:],
        }

        sT = E(nc.semaphore("sT"))
        sV = E(nc.semaphore("sV"))
        sS = E(nc.semaphore("sS"))
        sXx = E(nc.semaphore("sXx"))
        sXw = E(nc.semaphore("sXw"))
        sXd0 = E(nc.semaphore("sXd0"))
        sXd1 = E(nc.semaphore("sXd1"))
        sXd2 = E(nc.semaphore("sXd2"))
        sXq = E(nc.semaphore("sXq"))
        sXs0 = E(nc.semaphore("sXs0"))
        sXs1 = E(nc.semaphore("sXs1"))
        sST = E(nc.semaphore("sST"))
        SEMS = {"T": sT, "V": sV, "S": sS, "Xx": sXx, "Xw": sXw, "Xq": sXq,
                "Xd0": sXd0, "Xd1": sXd1, "Xd2": sXd2,
                "Xs0": sXs0, "Xs1": sXs1, "ST": sST}

        p = Planner()

        def gdma(out, in_, key="Xw"):
            sem = SEMS[key]
            def fn(eng):
                eng.dma_start(out=out, in_=in_).then_inc(sem, 16)
            p.do("gpsimd", fn)
            return p.inc(key, 16)

        def sdma_store(out, in_):
            def fn(eng):
                eng.dma_start(out=out, in_=in_).then_inc(sST, 16)
            p.do("sync", fn)
            return p.inc("ST", 16)

        def sdma_load(out, in_, key):
            sem = SEMS[key]
            def fn(eng):
                eng.dma_start(out=out, in_=in_).then_inc(sem, 16)
            p.do("sync", fn)
            return p.inc(key, 16)

        def mm(out, lhsT, rhs, start, stop, inc=False, tp=None):
            def fn(eng):
                r = eng.matmul(out, lhsT, rhs, start=start, stop=stop, tile_position=tp)
                if inc:
                    r.then_inc(sT, 1)
            p.do("tensor", fn)
            return p.inc("T") if inc else p.cnt["T"]

        def vop(fn_builder, inc=True):
            def fn(eng):
                r = fn_builder(eng)
                if inc:
                    r.then_inc(sV, 1)
            p.do("vector", fn)
            return p.inc("V") if inc else p.cnt["V"]

        def sop(fn_builder, inc=True, drain=False):
            def fn(eng):
                r = fn_builder(eng)
                if inc:
                    r.then_inc(sS, 1)
            p.do("scalar", fn)
            return p.inc("S") if inc else p.cnt["S"]

        def tap(name, buf, dep_key, dep_val):
            if not debug:
                return
            p.wait("sync", dep_key, dep_val)
            st_val = sdma_store(dbg[name], buf)
            for engk in ("tensor", "vector", "scalar", "gpsimd"):
                p.wait(engk, "ST", st_val)

        bank_release = {i: [] for i in range(8)}

        def twait_bank(b):
            for key, val in bank_release[b]:
                p.wait("tensor", key, val)
            bank_release[b] = []

        def set_release(b, *edges):
            bank_release[b] = [e for e in edges if e[1] > 0]

        v_wk = gdma(wk_bf.rearrange("p (c d) -> p c d", c=NCH),
                    wk_t.rearrange("(c p) d -> p c d", p=128))
        v_wv = gdma(wv_bf.rearrange("p (c d) -> p c d", c=NCH),
                    wv_t.rearrange("(c p) d -> p c d", p=128))
        v_bk = gdma(bias2, bias2_d)
        v_bv = gdma(smalls[0:2, 3584:3840], sel2_d)
        v_onec = gdma(onec_bf[:], onec_d)
        v_small = v_onec
        for dst, src in ((bq_pp[:], bq_d), (bo_pp[:], bo_d), (gam_pp[:], gam_d),
                         (bet_pp[:], bet_d), (fc1b_pp[:], fc1b_d), (fc2bh_pp[:], fc2b_d),
                         (oner_f, oner_d), (selb, sel_d)):
            v_small = gdma(dst, src)
        p.wait("vector", "Xw", p.cnt["Xw"])
        vop(lambda e: e.tensor_scalar_add(bq1_pp[:], bq_pp[:], 1.0), inc=False)
        vop(lambda e: e.tensor_scalar_mul(fc2bh_pp[:], fc2bh_pp[:], 0.5), inc=False)
        for b in range(2):
            vop(lambda e, b_=b: e.memset(
                vtm[b_].rearrange("p (pr c) -> p pr c", c=130)[:, :, 128:130], 1.0), inc=False)
        vop(lambda e: e.memset(kv_acc, 0.0), inc=False)
        v_setup = vop(lambda e: e.memset(row_c, 0.0))
        v_wq = gdma(wq_bf.rearrange("p (c d) -> p c d", c=NCH),
                    wq_t.rearrange("(c p) d -> p c d", p=128), key="Xq")
        v_wo = gdma(wo_bf.rearrange("p (c d) -> p c d", c=NCH),
                    wo_t.rearrange("(c p) d -> p c d", p=128), key="Xq")

        deq_release = {}

        deq_pieces = []
        for kc in range(NCH):
            for j in range(8):
                deq_pieces.append(("w1", kc, j))
        for kc in range(FCH):
            if kc not in W2_DEFER:
                for j in range(2):
                    deq_pieces.append(("w2", kc, j))
        N_NONDEF = len(deq_pieces)
        for kc in W2_DEFER:
            for j in range(2):
                deq_pieces.append(("w2", kc, j))
        deq = {"dma_i": 0, "stt_i": 0, "vals": [], "defer_gate": None}
        deq_dma_meta = []

        def _deq_srcs(which, kc, j):
            if which == "w1":
                return (q1_d[128 * kc:128 * (kc + 1), 512 * j:512 * (j + 1)],
                        sc1_d[0:1, F * kc + 512 * j: F * kc + 512 * (j + 1)],
                        w1_bf[:, F * kc + 512 * j: F * kc + 512 * (j + 1)])
            return (q2_d[128 * kc:128 * (kc + 1), 512 * j:512 * (j + 1)],
                    sc2_d[0:1, D * kc + 512 * j: D * kc + 512 * (j + 1)],
                    w2_bf[:, D * kc + 512 * j: D * kc + 512 * (j + 1)])

        def _deq_emit_dma():
            n = deq["dma_i"]
            deq["dma_i"] += 1
            which, kc, j = deq_pieces[n]
            par = n % 2
            src_int, src_sc, dst = _deq_srcs(which, kc, j)
            gate = deq["vals"][n - 2] if n >= 2 else 0
            if n == N_NONDEF and deq["defer_gate"] is not None:
                p.wait("gpsimd", "T", deq["defer_gate"])
            if gate:
                p.wait("gpsimd", "V", gate)
            xi = gdma(int_st[par], src_int, key="Xd%d" % par)
            if gate:
                p.wait("sync", "V", gate)
            xs = sdma_load(scbc[par], src_sc.partition_broadcast(128),
                           key="Xs%d" % par)
            deq_dma_meta.append((par, xi, xs, dst, which, kc))

        def _deq_emit_stt():
            n = deq["stt_i"]
            deq["stt_i"] += 1
            par, xi, xs, dst, which, kc = deq_dma_meta[n]
            p.wait("vector", "Xd%d" % par, xi)
            p.wait("vector", "Xs%d" % par, xs)
            v = vop(lambda e, pa=par, d=dst: e.scalar_tensor_tensor(
                out=d, in0=int_st[pa], scalar=-128.0, in1=scbc[pa],
                op0=OP.add, op1=OP.mult))
            deq["vals"].append(v)
            deq_release[(which, kc)] = v

        def deq_step(k, limit=None):
            for _ in range(k):
                lim = len(deq_pieces) if limit is None else limit
                while deq["dma_i"] < min(deq["stt_i"] + 2, lim):
                    _deq_emit_dma()
                if deq["stt_i"] < deq["dma_i"]:
                    _deq_emit_stt()

        xp1_release = [0, 0]
        x1_done = [0] * NT1

        def emit_p1_load(i):
            if i >= NT1:
                return
            if i >= 2:
                p.wait("gpsimd", "T", xp1_release[i % 2])
            x1_done[i] = gdma(
                xp1[i % 2].rearrange("p (c t) -> p c t", c=NCH),
                xt.rearrange("(c p) t -> p c t", p=128)[:, :, P1T * i:P1T * (i + 1)],
                key="Xx")

        emit_p1_load(0)
        emit_p1_load(1)

        p.wait("tensor", "Xw", p.cnt["Xw"])
        p.wait("tensor", "V", v_setup)

        NSUB_TOT = T // 128
        kv_fold = 0
        rel_k = [0, 0]
        rel_kS = 0
        rel_kV = 0
        rel_vV = 0
        ktm_ready = [0] * NSUB_TOT

        def kv_bank_col(pr):
            return (4 + pr // 3, 130 * (pr % 3)) if pr < 6 else (6, 130 * (pr - 6))

        for i in range(NT1):
            p.wait("tensor", "Xx", x1_done[i])
            for s in range(NSUB):
                g = i * NSUB + s
                par = g % 2
                p.wait("tensor", "S", rel_kS)
                p.wait("tensor", "V", rel_kV)
                for fh in range(2):
                    mm(P[fh][:, 0:512], sel2k, bias2[:, 512 * fh:512 * (fh + 1)],
                       start=True, stop=False)
                    for kc in range(NCH):
                        lhsT = xp1[i % 2][:, P1T * kc + 128 * s: P1T * kc + 128 * (s + 1)]
                        rhs = w2_bf[:, 16 * D + 1024 * kc + 512 * fh: 16 * D + 1024 * kc + 512 * (fh + 1)]
                        mm(P[fh][:, 0:512], lhsT, rhs, start=False, stop=(kc == NCH - 1),
                           inc=(kc == NCH - 1))
                tk = p.cnt["T"]
                p.wait("tensor", "V", rel_vV)
                for fh in range(2):
                    mm(P[2 + fh][:, 0:512], sel2v, bias2[:, 512 * fh:512 * (fh + 1)],
                       start=True, stop=False)
                    for kc in range(NCH):
                        lhsT = xp1[i % 2][:, P1T * kc + 128 * s: P1T * kc + 128 * (s + 1)]
                        rhs = w2_bf[:, 24 * D + 1024 * kc + 512 * fh: 24 * D + 1024 * kc + 512 * (fh + 1)]
                        mm(P[2 + fh][:, 0:512], lhsT, rhs, start=False, stop=(kc == NCH - 1),
                           inc=(kc == NCH - 1))
                tv = p.cnt["T"]
                if g >= 1:
                    gp = g - 1
                    p.wait("tensor", "V", ktm_ready[gp])
                    p.wait("tensor", "V", kv_fold)
                    for pr in range(8):
                        bank, col = kv_bank_col(pr)
                        mm(P[bank][:, col:col + 130], ktm[gp % 2][:, 128 * pr:128 * (pr + 1)],
                           vtm[gp % 2][:, 130 * pr:130 * (pr + 1)],
                           start=True, stop=True, inc=(pr == 7))
                    t_kvmm = p.cnt["T"]
                    p.wait("vector", "T", t_kvmm)
                    vop(lambda e: e.tensor_tensor(
                        out=kv_acc[:, 0:390], in0=kv_acc[:, 0:390],
                        in1=P[4][:, 0:390], op=OP.add), inc=False)
                    vop(lambda e: e.tensor_tensor(
                        out=kv_acc[:, 390:780], in0=kv_acc[:, 390:780],
                        in1=P[5][:, 0:390], op=OP.add), inc=False)
                    kv_fold = vop(lambda e: e.tensor_tensor(
                        out=kv_acc[:, 780:1040], in0=kv_acc[:, 780:1040],
                        in1=P[6][:, 0:260], op=OP.add))
                p.wait("scalar", "T", tk)
                se = [0, 0]
                for fh in range(2):
                    se[fh] = sop(lambda e, f=fh: e.activation(
                        out=p1esc[f], in_=P[f][:, 0:512], func=AF.Exp), drain=(fh == 1))
                rel_kS = se[1]
                p.wait("vector", "T", tk)
                vop(lambda e, pa=par: e.tensor_scalar_add(
                    ktm[pa][:, 0:512], P[0][:, 0:512], 1.0), inc=False)
                rel_kV = vop(lambda e, pa=par: e.tensor_scalar_add(
                    ktm[pa][:, 512:1024], P[1][:, 0:512], 1.0))
                for fh in range(2):
                    p.wait("vector", "S", se[fh])
                    vop(lambda e, f=fh: e.tensor_scalar_min(p1esc[f], p1esc[f], 1.0), inc=False)
                    vop(lambda e, f=fh, pa=par: e.tensor_tensor(
                        out=ktm[pa][:, 512 * f:512 * (f + 1)],
                        in0=ktm[pa][:, 512 * f:512 * (f + 1)],
                        in1=p1esc[f], op=OP.max), inc=False)
                p.wait("vector", "T", tv)
                for fh in range(2):
                    last = vop(lambda e, f=fh, pa=par: e.tensor_copy(
                        vtm[pa].rearrange("p (q c) -> p q c", c=130)[:, 4 * f:4 * (f + 1), 0:128],
                        P[2 + f][:, 0:512].rearrange("p (a c) -> p a c", c=128)),
                        inc=(fh == 1))
                rel_vV = last
                ktm_ready[g] = last
                deq_step(3, limit=N_NONDEF)
            xp1_release[i % 2] = p.cnt["T"]
            if i + 2 < NT1:
                emit_p1_load(i + 2)
        gp = NSUB_TOT - 1
        p.wait("tensor", "V", ktm_ready[gp])
        p.wait("tensor", "V", kv_fold)
        for pr in range(8):
            bank, col = kv_bank_col(pr)
            mm(P[bank][:, col:col + 130], ktm[gp % 2][:, 128 * pr:128 * (pr + 1)],
               vtm[gp % 2][:, 130 * pr:130 * (pr + 1)],
               start=True, stop=True, inc=(pr == 7))
        t_kv_done = p.cnt["T"]
        p.wait("vector", "T", t_kv_done)
        vop(lambda e: e.tensor_tensor(out=kv_acc[:, 0:390], in0=kv_acc[:, 0:390],
                                      in1=P[4][:, 0:390], op=OP.add), inc=False)
        vop(lambda e: e.tensor_tensor(out=kv_acc[:, 390:780], in0=kv_acc[:, 390:780],
                                      in1=P[5][:, 0:390], op=OP.add), inc=False)
        kv_fold = vop(lambda e: e.tensor_tensor(out=kv_acc[:, 780:1040],
                                                in0=kv_acc[:, 780:1040],
                                                in1=P[6][:, 0:260], op=OP.add))

        if debug:
            tap("ktm0", ktm[1], "V", ktm_ready[1])
            tap("vtm0", vtm[1], "V", ktm_ready[1])

        for pr in range(8):
            base = 130 * pr
            vop(lambda e, b=base, pr_=pr: e.tensor_copy(
                kv_sb[0:64, HD * pr_:HD * pr_ + HD], kv_acc[0:64, b:b + 64]), inc=False)
            vop(lambda e, b=base, pr_=pr: e.tensor_copy(
                kv_sb[64:128, HD * pr_:HD * pr_ + HD], kv_acc[64:128, b + 64:b + 128]), inc=False)
        vop(lambda e: e.memset(ksum_bd[:], 0), inc=False)
        for c in range(NCH):
            base = 130 * c
            vop(lambda e, b=base, c_=c: e.tensor_copy(
                ksum_bd[0:64, 18 * c_:18 * c_ + 1], kv_acc[0:64, b + 128:b + 129]), inc=False)
            vop(lambda e, b=base, c_=c: e.tensor_copy(
                ksum_bd[64:128, 18 * c_ + 1:18 * c_ + 2], kv_acc[64:128, b + 128:b + 129]))
        v_kvready = p.cnt["V"]
        for b in range(8):
            set_release(b, ("V", v_kvready))
        tap("kv", kv_sb[:], "V", v_kvready)
        tap("ksbd", ksum_bd[:], "V", v_kvready)

        x2_done = [0] * NT2
        xbf_release = {"v": 0}

        def emit_p2_load(i):
            if i >= NT2:
                return
            p.wait("gpsimd", "V", xbf_release["v"])
            if i == 0:
                p.wait("gpsimd", "T", t_kv_done)
            x2_done[i] = gdma(
                xbf.rearrange("p (c t) -> p c t", c=NCH),
                xt.rearrange("(c p) t -> p c t", p=128)[:, :, TT * i:TT * (i + 1)],
                key="Xx")

        emit_p2_load(0)

        rot01 = {"idx": 0}
        esc_rel = [0, 0]

        ts = [dict() for _ in range(NT2)]

        def stage_qproj(t):
            p.wait("tensor", "Xx", p.cnt["Xx"])
            ev_s = [0] * NCH
            for fc in range(NCH):
                bank = rot01["idx"]
                rot01["idx"] ^= 1
                twait_bank(bank)
                for kc in range(NCH):
                    mm(P[bank][:, 0:256], wq_bf[:, D * kc + 128 * fc: D * kc + 128 * (fc + 1)],
                       xbf[:, TT * kc:TT * (kc + 1)], start=(kc == 0), stop=(kc == NCH - 1),
                       inc=(kc == NCH - 1))
                tq = p.cnt["T"]
                p.wait("scalar", "T", tq)
                sop(lambda e, b=bank, f=fc: e.activation(
                    out=sqb[:, TT * f:TT * (f + 1)], in_=P[b][:, 0:256], func=AF.Exp,
                    bias=bq_pp[:, f:f + 1]), inc=False)
                ev_s[fc] = sop(lambda e, b=bank, f=fc: e.activation(
                    out=qbf[:, TT * f:TT * (f + 1)], in_=P[b][:, 0:256], func=AF.Identity,
                    bias=bq1_pp[:, f:f + 1]), drain=True)
                set_release(bank, ("S", ev_s[fc]))
            p.wait("vector", "S", ev_s[NCH - 1])
            vop(lambda e: e.tensor_scalar_min(sqb, sqb, 1.0), inc=False)
            v_q2 = vop(lambda e: e.tensor_tensor(out=qbf, in0=qbf, in1=sqb, op=OP.max))
            ts[t]["q2"] = v_q2
            if debug and t == 0:
                tap("q2", qbf, "V", v_q2)

        def stage_norm(t):
            twait_bank(6)
            p.wait("tensor", "V", ts[t]["q2"])
            for kc in range(NCH):
                mm(P[6][0:16, 0:256], ksum_bd[:, 16 * kc:16 * (kc + 1)],
                   qbf[:, TT * kc:TT * (kc + 1)], start=(kc == 0), stop=(kc == NCH - 1),
                   inc=(kc == NCH - 1))
            t_norm = p.cnt["T"]
            p.wait("vector", "T", t_norm)
            vop(lambda e: e.tensor_scalar_max(P[6][0:16, 256:512], P[6][0:16, 0:256],
                                              EPS_ATTN), inc=False)
            vop(lambda e: e.reciprocal(out=P[6][0:16, 256:512],
                                       in_=P[6][0:16, 256:512]), inc=False)
            v_recip = vop(lambda e: e.tensor_copy(nrow_bf, P[6][0:16, 256:512]))
            ts[t]["recip"] = v_recip
            set_release(6, ("V", v_recip))
            if debug and t == 0:
                tap("nrow", nrow_bf, "V", v_recip)

        def stage_bcast_recip(t):
            v_q3 = [0] * NCH
            for c in range(NCH):
                bank = 6 + c % 2
                twait_bank(bank)
                if c < 2:
                    p.wait("tensor", "V", ts[t]["recip"])
                mm(P[bank][:, 0:256], selb[:, 128 * c:128 * (c + 1)], nrow_bf,
                   start=True, stop=True, inc=True)
                tb = p.cnt["T"]
                p.wait("vector", "T", tb)
                v_q3[c] = vop(lambda e, b=bank, c_=c: e.tensor_tensor(
                    out=qbf[:, TT * c_:TT * (c_ + 1)], in0=qbf[:, TT * c_:TT * (c_ + 1)],
                    in1=P[b][:, 0:256], op=OP.mult))
                set_release(bank, ("V", v_q3[c]))
            ts[t]["q3"] = v_q3
            if debug and t == 0:
                tap("q3", qbf, "V", v_q3[NCH - 1])

        def stage_qkv(t):
            ev_mg = [0] * NCH
            for c in range(NCH):
                bank = 6 + c % 2
                twait_bank(bank)
                p.wait("tensor", "V", ts[t]["q3"][c])
                mm(P[bank][0:64, 0:256], kv_sb[0:64, HD * c:HD * (c + 1)],
                   qbf[0:64, TT * c:TT * (c + 1)], start=True, stop=True)
                mm(P[bank][64:128, 0:256], kv_sb[64:128, HD * c:HD * (c + 1)],
                   qbf[64:128, TT * c:TT * (c + 1)], start=True, stop=True,
                   inc=True, tp=(64, 64))
                tm = p.cnt["T"]
                p.wait("scalar", "T", tm)
                ev_mg[c] = sop(lambda e, b=bank, c_=c: e.activation(
                    out=mgbf[:, TT * c_:TT * (c_ + 1)], in_=P[b][:, 0:256], func=AF.Copy),
                    drain=True)
                set_release(bank, ("S", ev_mg[c]))
            ts[t]["mg"] = ev_mg[NCH - 1]
            if debug and t == 0:
                tap("mg", mgbf, "S", ev_mg[NCH - 1])

        def stage_attnproj(t, lo=0, hi=NCH):
            ev_r1 = ts[t].setdefault("r1s", [0] * NCH)
            if lo == 0:
                p.wait("tensor", "S", ts[t]["mg"])
            for fc in range(lo, hi):
                bank = rot01["idx"]
                rot01["idx"] ^= 1
                twait_bank(bank)
                for kc in range(NCH):
                    mm(P[bank][:, 0:256], wo_bf[:, D * kc + 128 * fc: D * kc + 128 * (fc + 1)],
                       mgbf[:, TT * kc:TT * (kc + 1)], start=(kc == 0), stop=(kc == NCH - 1),
                       inc=(kc == NCH - 1))
                ta = p.cnt["T"]
                p.wait("vector", "T", ta)
                ev_r1[fc] = vop(lambda e, b=bank, f=fc: e.scalar_tensor_tensor(
                    out=r1bf[:, TT * f:TT * (f + 1)], in0=P[b][:, 0:256],
                    scalar=bo_pp[:, f:f + 1], in1=xbf[:, TT * f:TT * (f + 1)],
                    op0=OP.add, op1=OP.add))
                set_release(bank, ("V", ev_r1[fc]))
            if hi == NCH:
                ts[t]["r1"] = ev_r1[NCH - 1]
                xbf_release["v"] = ev_r1[NCH - 1]
                if debug and t == 0:
                    tap("r1", r1bf, "V", ev_r1[NCH - 1])
                emit_p2_load(t + 1)

        def ln_rows(src_bf, scale_m, scale_sq, v_dep):
            v_sq = vop(lambda e: e.tensor_tensor(out=sqb, in0=src_bf, in1=src_bf, op=OP.mult))
            twait_bank(6)
            twait_bank(7)
            p.wait("tensor", "V", v_sq)
            for pp in range(4):
                mm(P[6][0:1, 0:512], onec_bf[:], src_bf[:, 512 * pp:512 * (pp + 1)],
                   start=(pp == 0), stop=(pp == 3))
            for pp in range(4):
                mm(P[7][0:1, 0:512], onec_bf[:], sqb[:, 512 * pp:512 * (pp + 1)],
                   start=(pp == 0), stop=(pp == 3), inc=(pp == 3))
            t_st = p.cnt["T"]
            p.wait("vector", "T", t_st)
            vop(lambda e: e.tensor_scalar_mul(row_b, P[6][0:1, 0:256], scale_m), inc=False)
            vop(lambda e: e.scalar_tensor_tensor(
                out=row_b, in0=P[6][0:1, 256:512], scalar=scale_m, in1=row_b,
                op0=OP.mult, op1=OP.add), inc=False)
            vop(lambda e: e.tensor_scalar(out=row_c, in0=P[7][0:1, 0:256],
                                          scalar1=scale_sq, scalar2=EPS_LN,
                                          op0=OP.mult, op1=OP.add), inc=False)
            vop(lambda e: e.scalar_tensor_tensor(
                out=row_c, in0=P[7][0:1, 256:512], scalar=scale_sq, in1=row_c,
                op0=OP.mult, op1=OP.add), inc=False)
            vop(lambda e: e.tensor_tensor(out=P[6][0:1, 0:256], in0=row_b, in1=row_b,
                                          op=OP.mult), inc=False)
            v_var = vop(lambda e: e.tensor_tensor(out=row_c, in0=row_c,
                                                  in1=P[6][0:1, 0:256], op=OP.subtract))
            p.wait("scalar", "V", v_var)
            s_sqrt = sop(lambda e: e.activation(out=row_c, in_=row_c, func=AF.Sqrt))
            p.wait("vector", "S", s_sqrt)
            v_rstd = vop(lambda e: e.reciprocal(out=row_c, in_=row_c))
            set_release(6, ("V", v_rstd))
            return v_rstd

        def ln_bcast_norm(src_bf, dst_bf, a_scale, v_rows):
            va = v_rows
            if a_scale != 1.0:
                va = vop(lambda e: e.tensor_scalar_mul(row_c, row_c, a_scale))
            vb = vop(lambda e: e.scalar_tensor_tensor(
                out=row_b, in0=row_b, scalar=-1.0, in1=row_c, op0=OP.mult, op1=OP.mult))
            p.wait("tensor", "V", max(va, vb))
            mm(P[7][:, 0:256], oner_f, row_c, start=True, stop=True)
            mm(P[7][:, 256:512], oner_f, row_b, start=True, stop=True, inc=True)
            t_bc = p.cnt["T"]
            p.wait("vector", "T", t_bc)
            a_tile = P[7][:, 0:256]
            b_tile = P[7][:, 256:512]
            a_rep = bass.AP(a_tile.tensor, a_tile.offset, [a_tile.ap[0], [0, NCH], a_tile.ap[1]])
            b_rep = bass.AP(b_tile.tensor, b_tile.offset, [b_tile.ap[0], [0, NCH], b_tile.ap[1]])
            src3 = src_bf.rearrange("p (c t) -> p c t", c=NCH)
            dst3 = dst_bf.rearrange("p (c t) -> p c t", c=NCH)
            vop(lambda e: e.tensor_tensor(out=dst3, in0=src3, in1=a_rep, op=OP.mult), inc=False)
            v_done = vop(lambda e: e.tensor_tensor(out=dst3, in0=dst3, in1=b_rep, op=OP.add))
            set_release(7, ("V", v_done))
            return v_done

        def stage_ln1_rows(t):
            p.wait("vector", "V", ts[t]["r1"])
            ts[t]["ln1rows"] = ln_rows(r1bf, 1.0 / D, 1.0 / D, ts[t]["r1"])

        def stage_ln1_bcast(t):
            v_x2 = ln_bcast_norm(r1bf, x2s[t % 2], 2.0, ts[t]["ln1rows"])
            ts[t]["x2"] = v_x2
            if debug and t == 0:
                tap("x2", x2s[t % 2], "V", v_x2)

        def stage_ln2_rows(t):
            ts[t]["ln2rows"] = ln_rows(x2s[t % 2], 1.0 / (2 * D), 1.0 / (4 * D),
                                       ts[t]["x2"])

        def stage_ln2_bcast(t):
            v_hraw = ln_bcast_norm(x2s[t % 2], hbf, 0.5, ts[t]["ln2rows"])
            v_h = 0
            for c in range(NCH):
                v_h = vop(lambda e, c_=c: e.tensor_scalar(
                    out=hbf[:, TT * c_:TT * (c_ + 1)], in0=hbf[:, TT * c_:TT * (c_ + 1)],
                    scalar1=gam_pp[:, c_:c_ + 1], scalar2=bet_pp[:, c_:c_ + 1],
                    op0=OP.mult, op1=OP.add))
            ts[t]["h"] = v_h
            if debug and t == 0:
                tap("h", hbf, "V", v_h)

        def stage_fc1(f, fc0, fc1_):
            if f == 0 and fc0 == 0:
                p.wait("tensor", "V", max(deq_release[("w1", kc)] for kc in range(NCH)))
            ev_y = ts[f].setdefault("y", {})
            for fc in range(fc0, fc1_):
                j = fc % 16
                bank = rot01["idx"]
                rot01["idx"] ^= 1
                twait_bank(bank)
                if fc == 0:
                    p.wait("tensor", "V", ts[f]["h"])
                for kc in range(NCH):
                    mm(P[bank][:, 0:256],
                       w1_bf[:, F * kc + 128 * fc: F * kc + 128 * (fc + 1)],
                       hbf[:, TT * kc:TT * (kc + 1)], start=(kc == 0), stop=(kc == NCH - 1),
                       inc=(kc == NCH - 1))
                tf = p.cnt["T"]
                p.wait("scalar", "T", tf)
                if fc == fc0:
                    p.wait("scalar", "ST", p.cnt["ST"])
                ev_y[fc] = sop(lambda e, b=bank, fcx=fc, j_=j: e.activation(
                    out=ybf[:, TT * j_:TT * (j_ + 1)], in_=P[b][:, 0:256], func=AF.Silu,
                    bias=fc1b_pp[:, fcx:fcx + 1]), drain=True)
                set_release(bank, ("S", ev_y[fc]))
            if debug and f == 0 and fc1_ == 16:
                tap("y0", ybf, "S", ev_y[15])

        def stage_fc2(f, half):
            ev_y = ts[f]["y"]
            for zbank in range(2, 6):
                twait_bank(zbank)
                for sub in range(2):
                    fco = 2 * (zbank - 2) + sub
                    zcol = 256 * sub
                    for j in range(16):
                        kc = 16 * half + j
                        if fco == 0:
                            p.wait("tensor", "S", ev_y[kc])
                        if f == 0 and zbank == 2 and sub == 0 and ("w2", kc) in deq_release:
                            p.wait("tensor", "V", deq_release[("w2", kc)])
                        mm(P[zbank][:, zcol:zcol + 256],
                           w2_bf[:, D * kc + 128 * fco: D * kc + 128 * (fco + 1)],
                           ybf[:, TT * j:TT * (j + 1)],
                           start=(j == 0), stop=(j == 15),
                           inc=(j == 15))
                t_z = p.cnt["T"]
                if half == 0:
                    p.wait("scalar", "T", t_z)
                    if zbank == 2 and f + 1 < NT2 and "x2" in ts[f + 1]:
                        p.wait("scalar", "V", ts[f + 1]["x2"])
                    sz = 0
                    for sub in range(2):
                        fco = 2 * (zbank - 2) + sub
                        sz = sop(lambda e, b=zbank, c=256 * sub, f_=fco: e.activation(
                            out=zacc[:, TT * f_:TT * (f_ + 1)], in_=P[b][:, c:c + 256],
                            func=AF.Copy))
                    bank_release[zbank] = [("S", sz)]
                else:
                    p.wait("scalar", "T", t_z)
                    x2w = x2s[f % 2]
                    ev_w = 0
                    s_last = 0
                    for sub in range(2):
                        fco = 2 * (zbank - 2) + sub
                        slot = fco % len(esc)
                        if esc_rel[slot]:
                            p.wait("scalar", "V", esc_rel[slot])
                        s_last = sop(lambda e, b=zbank, c=256 * sub, f_=fco: e.activation(
                            out=esc[f_ % len(esc)], in_=P[b][:, c:c + 256],
                            func=AF.Identity, bias=fc2bh_pp[:, f_:f_ + 1], scale=0.5))
                        p.wait("vector", "S", s_last)
                        vop(lambda e, f_=fco: e.scalar_tensor_tensor(
                            out=esc[f_ % len(esc)], in0=zacc[:, TT * f_:TT * (f_ + 1)],
                            scalar=0.5, in1=esc[f_ % len(esc)],
                            op0=OP.mult, op1=OP.add), inc=False)
                        ev_w = vop(lambda e, f_=fco: e.tensor_tensor(
                            out=x2w[:, TT * f_:TT * (f_ + 1)], in0=esc[f_ % len(esc)],
                            in1=x2w[:, TT * f_:TT * (f_ + 1)], op=OP.add))
                        esc_rel[slot] = ev_w
                    bank_release[zbank] = [("S", s_last)]
                    ts[f]["w"] = ev_w
            ts[f]["fc2_t"] = p.cnt["T"]

        def stage_ln3_rows(f):
            x2w = x2s[f % 2]
            v_rows = ln_rows(x2w, 1.0 / D, 1.0 / D, ts[f]["w"])
            vb = vop(lambda e: e.scalar_tensor_tensor(
                out=row_b, in0=row_b, scalar=-1.0, in1=row_c, op0=OP.mult, op1=OP.mult))
            ts[f]["ln3rows"] = vb

        def stage_ln3_store(f):
            x2w = x2s[f % 2]
            twait_bank(7)
            p.wait("tensor", "V", ts[f]["ln3rows"])
            mm(P[7][:, 0:256], oner_f, row_c, start=True, stop=True)
            mm(P[7][:, 256:512], oner_f, row_b, start=True, stop=True, inc=True)
            t_bc3 = p.cnt["T"]
            p.wait("vector", "T", t_bc3)
            a_tile = P[7][:, 0:256]
            b_tile = P[7][:, 256:512]
            a_rep = bass.AP(a_tile.tensor, a_tile.offset, [a_tile.ap[0], [0, NCH], a_tile.ap[1]])
            b_rep = bass.AP(b_tile.tensor, b_tile.offset, [b_tile.ap[0], [0, NCH], b_tile.ap[1]])
            src3 = x2w.rearrange("p (c t) -> p c t", c=NCH)
            dst3 = outfull.rearrange("p (c t) -> p c t", c=NCH)
            vop(lambda e: e.tensor_tensor(out=dst3, in0=src3, in1=a_rep, op=OP.mult), inc=False)
            vlast = vop(lambda e: e.tensor_tensor(out=dst3, in0=dst3, in1=b_rep, op=OP.add))
            p.wait("sync", "V", vlast)
            sdma_store(
                out_t.rearrange("(c p) t -> p c t", p=128)[:, :, TT * f:TT * (f + 1)],
                dst3)
            set_release(7, ("V", vlast))

        if p1_only:
            if debug and dbg.get("_qs17"):
                pass
            p.wait("sync", "ST", p.cnt["ST"])
            with nc.Block() as block:
                def runner(events):
                    def _(eng):
                        for item in events:
                            if item[0] == "do":
                                item[1](eng)
                            else:
                                _, key, val = item
                                eng.wait_ge(SEMS[key], val)
                    return _
                block.tensor(runner(p.ev["tensor"]))
                block.vector(runner(p.ev["vector"]))
                block.scalar(runner(p.ev["scalar"]))
                block.sync(runner(p.ev["sync"]))
                block.gpsimd(runner(p.ev["gpsimd"]))
            return nc

        p.wait("tensor", "Xq", p.cnt["Xq"])
        if prol >= 1:
            stage_qproj(0)
        if prol >= 2:
            stage_norm(0)
        if prol >= 3:
            stage_bcast_recip(0)
        if prol >= 4:
            stage_qkv(0)

        deq["defer_gate"] = t_kv_done
        deq_step(6)

        n_iters = NT2 + 1 if max_iters is None else max_iters
        for it in range(n_iters):
            if it <= 2:
                deq_step(13)
            if it < NT2:
                stage_attnproj(it, 0, 4)
            if it >= 2:
                stage_ln3_store(it - 2)
            if it < NT2:
                stage_attnproj(it, 4, NCH)
            if it >= 1:
                stage_fc1(it - 1, 0, 8)
            if it < NT2:
                stage_ln1_rows(it)
            if it >= 1:
                stage_fc1(it - 1, 8, 16)
            if it < NT2:
                stage_ln1_bcast(it)
            if it + 1 < NT2:
                stage_qproj(it + 1)
            if it >= 1:
                stage_fc2(it - 1, 0)
            if it + 1 < NT2:
                stage_norm(it + 1)
            if it >= 1:
                stage_fc1(it - 1, 16, 24)
            if it + 1 < NT2:
                stage_bcast_recip(it + 1)
            if it >= 1:
                stage_fc1(it - 1, 24, 32)
            if it == 0:
                deq_step(len(deq_pieces))
            if it < NT2:
                stage_ln2_rows(it)
                stage_ln2_bcast(it)
            if it >= 1:
                stage_fc2(it - 1, 1)
            if it + 1 < NT2:
                stage_qkv(it + 1)
            if it >= 1:
                stage_ln3_rows(it - 1)

        if NT2 >= 1 and (max_iters is None or max_iters > NT2):
            stage_ln3_store(NT2 - 1)
        p.wait("sync", "ST", p.cnt["ST"])

        with nc.Block() as block:
            def runner(events):
                def _(eng):
                    for item in events:
                        if item[0] == "do":
                            item[1](eng)
                        else:
                            _, key, val = item
                            eng.wait_ge(SEMS[key], val)
                return _
            block.tensor(runner(p.ev["tensor"]))
            block.vector(runner(p.ev["vector"]))
            block.scalar(runner(p.ev["scalar"]))
            block.sync(runner(p.ev["sync"]))
            block.gpsimd(runner(p.ev["gpsimd"]))

    return nc


_NC_CACHE = None


def prep_in_maps(L0_attn_x, L0_attn_Wq, L0_attn_bq, L0_attn_Wk, L0_attn_bk,
                 L0_attn_Wv, L0_attn_bv, L0_attn_Wo, L0_attn_bo,
                 L0_ffn_ln_gamma, L0_ffn_ln_beta, L0_ffn_fc1_q, L0_ffn_fc1_scales,
                 L0_ffn_fc1_bias, L0_ffn_fc2_q, L0_ffn_fc2_scales, L0_ffn_fc2_bias):
    asc = np.ascontiguousarray
    common = {
        "wq_t": asc(np.asarray(L0_attn_Wq, np.float32).T),
        "wk_t": asc(np.asarray(L0_attn_Wk, np.float32).T),
        "wv_t": asc(np.asarray(L0_attn_Wv, np.float32).T),
        "wo_t": asc(np.asarray(L0_attn_Wo, np.float32).T),
        "bq_d": asc(np.asarray(L0_attn_bq, np.float32).reshape(NCH, 128).T),
        "bo_d": asc(np.asarray(L0_attn_bo, np.float32).reshape(NCH, 128).T),
        "bias2_d": asc(np.stack([np.asarray(L0_attn_bk, np.float32),
                                 np.asarray(L0_attn_bv, np.float32)], axis=0)),
        "gam_d": asc(np.asarray(L0_ffn_ln_gamma, np.float32).reshape(NCH, 128).T),
        "bet_d": asc(np.asarray(L0_ffn_ln_beta, np.float32).reshape(NCH, 128).T),
        "fc1b_d": asc(np.asarray(L0_ffn_fc1_bias, np.float32).reshape(FCH, 128).T),
        "fc2b_d": asc(np.asarray(L0_ffn_fc2_bias, np.float32).reshape(NCH, 128).T),
        "q1_d": asc(np.asarray(L0_ffn_fc1_q, np.int32)),
        "q2_d": asc(np.asarray(L0_ffn_fc2_q, np.int32)),
        "sc1_d": asc(np.asarray(L0_ffn_fc1_scales, np.float32).reshape(1, NCH * F)),
        "sc2_d": asc(np.asarray(L0_ffn_fc2_scales, np.float32).reshape(1, FCH * D)),
    }
    sel = np.zeros((16, D), np.float32)
    for c in range(NCH):
        sel[2 * c, 128 * c:128 * c + 64] = 1.0
        sel[2 * c + 1, 128 * c + 64:128 * c + 128] = 1.0
    common["sel_d"] = sel
    sel2 = np.zeros((2, 256), np.float32)
    sel2[0, 0:128] = 1.0
    sel2[1, 128:256] = 1.0
    common["sel2_d"] = sel2
    common["onec_d"] = np.ones((128, 1), np.float32)
    common["oner_d"] = np.ones((1, 128), np.float32)

    x = np.asarray(L0_attn_x, np.float32)
    in_maps = []
    for b in range(8):
        m = dict(common)
        m["xt"] = asc(x[b].T)
        in_maps.append(m)
    return in_maps


def kernel(**inputs):
    global _NC_CACHE
    if _NC_CACHE is None:
        _NC_CACHE = build_nc()
    nc = _NC_CACHE
    in_maps = prep_in_maps(**inputs)
    res = run_bass_kernel_spmd(nc, in_maps, core_ids=list(range(8)))
    out = np.stack([np.ascontiguousarray(r["out_t"].T) for r in res.results], axis=0)
    return out


# version 23
# speedup vs baseline: 1.0377x; 1.0377x over previous
import numpy as np
from contextlib import ExitStack

import concourse.bass as bass
import concourse.mybir as mybir
from concourse.bass_utils import run_bass_kernel_spmd

F32 = mybir.dt.float32
BF16 = mybir.dt.bfloat16
I32 = mybir.dt.int32
AF = mybir.ActivationFunctionType
OP = mybir.AluOpType

D = 1024
T = 4096
NCH = 8
HD = 64
F = 4096
FCH = 32
TT = 256
NT2 = T // TT
P1T = 512
NT1 = T // P1T
NSUB = P1T // 128
EPS_LN = 1e-5
EPS_ATTN = 1e-6

W1_DEFER = ()
W2_DEFER = tuple(range(16, 32))


class Planner:
    ENGINES = ("tensor", "vector", "scalar", "sync", "gpsimd")

    def __init__(self):
        self.ev = {e: [] for e in self.ENGINES}
        self.cnt = {"T": 0, "V": 0, "S": 0, "Xx": 0, "Xw": 0, "Xq": 0,
                    "Xd0": 0, "Xd1": 0, "Xd2": 0, "Xs0": 0, "Xs1": 0, "ST": 0}

    def do(self, eng, fn):
        self.ev[eng].append(("do", fn))

    def wait(self, eng, key, val):
        if val > 0:
            self.ev[eng].append(("wait", key, val))

    def inc(self, key, n=1):
        self.cnt[key] += n
        return self.cnt[key]


def build_nc(debug=False, p1_only=False, max_iters=None, prol=4, qsub=5):
    nc = bass.Bass("TRN2", target_bir_lowering=False)

    xt = nc.dram_tensor("xt", [D, T], F32, kind="ExternalInput").ap()
    wq_t = nc.dram_tensor("wq_t", [D, D], F32, kind="ExternalInput").ap()
    wk_t = nc.dram_tensor("wk_t", [D, D], F32, kind="ExternalInput").ap()
    wv_t = nc.dram_tensor("wv_t", [D, D], F32, kind="ExternalInput").ap()
    wo_t = nc.dram_tensor("wo_t", [D, D], F32, kind="ExternalInput").ap()
    bq_d = nc.dram_tensor("bq_d", [128, NCH], F32, kind="ExternalInput").ap()
    bo_d = nc.dram_tensor("bo_d", [128, NCH], F32, kind="ExternalInput").ap()
    bias2_d = nc.dram_tensor("bias2_d", [2, D], F32, kind="ExternalInput").ap()
    sel2_d = nc.dram_tensor("sel2_d", [2, 256], F32, kind="ExternalInput").ap()
    gam_d = nc.dram_tensor("gam_d", [128, NCH], F32, kind="ExternalInput").ap()
    bet_d = nc.dram_tensor("bet_d", [128, NCH], F32, kind="ExternalInput").ap()
    fc1b_d = nc.dram_tensor("fc1b_d", [128, FCH], F32, kind="ExternalInput").ap()
    fc2b_d = nc.dram_tensor("fc2b_d", [128, NCH], F32, kind="ExternalInput").ap()
    q1_d = nc.dram_tensor("q1_d", [D, F], I32, kind="ExternalInput").ap()
    q2_d = nc.dram_tensor("q2_d", [F, D], I32, kind="ExternalInput").ap()
    sc1_d = nc.dram_tensor("sc1_d", [1, NCH * F], F32, kind="ExternalInput").ap()
    sc2_d = nc.dram_tensor("sc2_d", [1, FCH * D], F32, kind="ExternalInput").ap()
    sel_d = nc.dram_tensor("sel_d", [16, D], F32, kind="ExternalInput").ap()
    onec_d = nc.dram_tensor("onec_d", [128, 1], F32, kind="ExternalInput").ap()
    oner_d = nc.dram_tensor("oner_d", [1, 128], F32, kind="ExternalInput").ap()
    out_t = nc.dram_tensor("out_t", [D, T], F32, kind="ExternalOutput").ap()

    dbg = {}
    if debug:
        for name, shape, dt in [("kv", [128, NCH * HD], BF16), ("ksbd", [128, 128], BF16),
                                ("q2", [128, 2048], BF16), ("nrow", [16, TT], BF16),
                                ("q3", [128, 2048], BF16), ("mg", [128, 2048], BF16),
                                ("r1", [128, 2048], BF16), ("x2", [128, 2048], BF16),
                                ("h", [128, 2048], BF16), ("y0", [128, 4096], BF16),
                                ("w", [128, 2048], BF16), ("ktm0", [128, 1024], BF16),
                                ("vtm0", [128, 1040], BF16)]:
            dbg[name] = nc.dram_tensor("dbg_" + name, shape, dt, kind="ExternalOutput").ap()

    with ExitStack() as stack:
        E = stack.enter_context
        wq_bf = E(nc.sbuf_tensor("wq_bf", [128, NCH * D], BF16))
        wo_bf = E(nc.sbuf_tensor("wo_bf", [128, NCH * D], BF16))
        w1_bf = E(nc.sbuf_tensor("w1_bf", [128, NCH * F], BF16))
        w2_bf = E(nc.sbuf_tensor("w2_bf", [128, FCH * D], BF16))
        kv_sb = E(nc.sbuf_tensor("kv_sb", [128, NCH * HD], BF16))
        ksum_bd = E(nc.sbuf_tensor("ksum_bd", [128, NCH * 16], BF16))
        onec_bf = E(nc.sbuf_tensor("onec_bf", [128, 1], BF16))
        bq_pp = E(nc.sbuf_tensor("bq_pp", [128, NCH], F32))
        bq1_pp = E(nc.sbuf_tensor("bq1_pp", [128, NCH], F32))
        ones_pp = E(nc.sbuf_tensor("ones_pp", [128, 1], F32))
        bo_pp = E(nc.sbuf_tensor("bo_pp", [128, NCH], F32))
        gam_pp = E(nc.sbuf_tensor("gam_pp", [128, NCH], F32))
        bet_pp = E(nc.sbuf_tensor("bet_pp", [128, NCH], F32))
        fc1b_pp = E(nc.sbuf_tensor("fc1b_pp", [128, FCH], F32))
        fc2bh_pp = E(nc.sbuf_tensor("fc2bh_pp", [128, NCH], F32))

        smalls = E(nc.sbuf_tensor("smalls", [128, 3840], BF16))
        selb = smalls[0:16, 0:1024]
        nrow_bf = smalls[0:16, 1024:1280]
        row_c = smalls[0:1, 1280:1792].bitcast(F32)
        row_b = smalls[0:1, 1792:2304].bitcast(F32)
        oner_f = smalls[0:1, 2304:2560].bitcast(F32)
        bias2 = smalls[0:2, 2560:3584]
        sel2k = smalls[0:2, 3584:3712]
        sel2v = smalls[0:2, 3712:3840]

        ACOLS = 19712
        arena = E(nc.sbuf_tensor("arena", [128, ACOLS], BF16))
        xbf = arena[:, 0:2048]
        qbf = arena[:, 2048:4096]
        esc = [arena[:, 4096:4352], arena[:, 18688:18944]]
        sqb = arena[:, 4352:6400]
        r1bf = arena[:, 6400:8448]
        x2s = [arena[:, 8448:10496], arena[:, 10496:12544]]
        hbf = arena[:, 12544:14592]
        ybf = arena[:, 14592:18688]
        mgbf = qbf
        outfull = arena[:, 14592:18688].bitcast(F32)
        xp1 = [arena[:, 0:4096], arena[:, 4096:8192]]
        int_st = [arena[:, 10496:11008], arena[:, 12032:12544],
                  arena[:, 18944:19456]]
        scbc = [arena[:, 11008:12032].bitcast(F32),
                arena[:, 13568:14592].bitcast(F32)]
        p1esc = [arena[:, 12544 + 512 * i:13056 + 512 * i] for i in range(2)]
        kv_acc = arena[:, 8192:10272].bitcast(F32)
        zacc = arena[:, 6400:8448]
        ktm = [arena[:, 14592:15616], arena[:, 15616:16640]]
        vtm = [arena[:, 16640 + 1040 * i:17680 + 1040 * i] for i in range(2)]
        wk_bf = w2_bf[:, 16 * D:24 * D]
        wv_bf = w2_bf[:, 24 * D:32 * D]

        P = [E(nc.psum_tensor(f"P{i}", [128, 512], F32)) for i in range(8)]
        global LAST_VIEWS
        LAST_VIEWS = {
            "r1bf": r1bf, "sqb": sqb, "row_c": row_c, "row_b": row_b,
            "qbf": qbf, "xbf": xbf, "mgbf": mgbf, "kv_acc": kv_acc,
            "P6": P[6][:], "P7": P[7][:], "kv_sb": kv_sb[:], "ksum_bd": ksum_bd[:],
            "x2a": x2s[0], "hbf": hbf, "esc": esc[0], "nrow_bf": nrow_bf,
            "out_t": out_t, "w1_bf": w1_bf[:], "w2_bf": w2_bf[:],
        }

        sT = E(nc.semaphore("sT"))
        sV = E(nc.semaphore("sV"))
        sS = E(nc.semaphore("sS"))
        sXx = E(nc.semaphore("sXx"))
        sXw = E(nc.semaphore("sXw"))
        sXd0 = E(nc.semaphore("sXd0"))
        sXd1 = E(nc.semaphore("sXd1"))
        sXd2 = E(nc.semaphore("sXd2"))
        sXq = E(nc.semaphore("sXq"))
        sXs0 = E(nc.semaphore("sXs0"))
        sXs1 = E(nc.semaphore("sXs1"))
        sST = E(nc.semaphore("sST"))
        SEMS = {"T": sT, "V": sV, "S": sS, "Xx": sXx, "Xw": sXw, "Xq": sXq,
                "Xd0": sXd0, "Xd1": sXd1, "Xd2": sXd2,
                "Xs0": sXs0, "Xs1": sXs1, "ST": sST}

        p = Planner()

        def gdma(out, in_, key="Xw"):
            sem = SEMS[key]
            def fn(eng):
                eng.dma_start(out=out, in_=in_).then_inc(sem, 16)
            p.do("gpsimd", fn)
            return p.inc(key, 16)

        def sdma_store(out, in_):
            def fn(eng):
                eng.dma_start(out=out, in_=in_).then_inc(sST, 16)
            p.do("sync", fn)
            return p.inc("ST", 16)

        def sdma_load(out, in_, key):
            sem = SEMS[key]
            def fn(eng):
                eng.dma_start(out=out, in_=in_).then_inc(sem, 16)
            p.do("sync", fn)
            return p.inc(key, 16)

        def mm(out, lhsT, rhs, start, stop, inc=False, tp=None):
            def fn(eng):
                r = eng.matmul(out, lhsT, rhs, start=start, stop=stop, tile_position=tp)
                if inc:
                    r.then_inc(sT, 1)
            p.do("tensor", fn)
            return p.inc("T") if inc else p.cnt["T"]

        def vop(fn_builder, inc=True):
            def fn(eng):
                r = fn_builder(eng)
                if inc:
                    r.then_inc(sV, 1)
            p.do("vector", fn)
            return p.inc("V") if inc else p.cnt["V"]

        def sop(fn_builder, inc=True, drain=False):
            def fn(eng):
                r = fn_builder(eng)
                if inc:
                    r.then_inc(sS, 1)
            p.do("scalar", fn)
            return p.inc("S") if inc else p.cnt["S"]

        def tap(name, buf, dep_key, dep_val):
            if not debug:
                return
            p.wait("sync", dep_key, dep_val)
            st_val = sdma_store(dbg[name], buf)
            for engk in ("tensor", "vector", "scalar", "gpsimd"):
                p.wait(engk, "ST", st_val)

        bank_release = {i: [] for i in range(8)}

        def twait_bank(b):
            for key, val in bank_release[b]:
                p.wait("tensor", key, val)
            bank_release[b] = []

        def set_release(b, *edges):
            bank_release[b] = [e for e in edges if e[1] > 0]

        v_wk = gdma(wk_bf.rearrange("p (c d) -> p c d", c=NCH),
                    wk_t.rearrange("(c p) d -> p c d", p=128))
        v_wv = gdma(wv_bf.rearrange("p (c d) -> p c d", c=NCH),
                    wv_t.rearrange("(c p) d -> p c d", p=128))
        v_bk = gdma(bias2, bias2_d)
        v_bv = gdma(smalls[0:2, 3584:3840], sel2_d)
        v_onec = gdma(onec_bf[:], onec_d)
        v_small = v_onec
        for dst, src in ((bq_pp[:], bq_d), (bo_pp[:], bo_d), (gam_pp[:], gam_d),
                         (bet_pp[:], bet_d), (fc1b_pp[:], fc1b_d), (fc2bh_pp[:], fc2b_d),
                         (oner_f, oner_d), (selb, sel_d)):
            v_small = gdma(dst, src)
        p.wait("vector", "Xw", p.cnt["Xw"])
        vop(lambda e: e.tensor_scalar_add(bq1_pp[:], bq_pp[:], 1.0), inc=False)
        vop(lambda e: e.tensor_scalar_mul(fc2bh_pp[:], fc2bh_pp[:], 0.5), inc=False)
        for b in range(2):
            vop(lambda e, b_=b: e.memset(
                vtm[b_].rearrange("p (pr c) -> p pr c", c=130)[:, :, 128:130], 1.0), inc=False)
        vop(lambda e: e.memset(kv_acc, 0.0), inc=False)
        vop(lambda e: e.memset(ones_pp[:], 1.0), inc=False)
        v_setup = vop(lambda e: e.memset(row_c, 0.0))
        v_wq = gdma(wq_bf.rearrange("p (c d) -> p c d", c=NCH),
                    wq_t.rearrange("(c p) d -> p c d", p=128), key="Xq")
        v_wo = gdma(wo_bf.rearrange("p (c d) -> p c d", c=NCH),
                    wo_t.rearrange("(c p) d -> p c d", p=128), key="Xq")

        deq_release = {}

        deq_pieces = []
        for kc in range(NCH):
            for j in range(8):
                deq_pieces.append(("w1", kc, j))
        for kc in range(FCH):
            if kc not in W2_DEFER:
                for j in range(2):
                    deq_pieces.append(("w2", kc, j))
        N_NONDEF = len(deq_pieces)
        for kc in W2_DEFER:
            for j in range(2):
                deq_pieces.append(("w2", kc, j))
        deq = {"dma_i": 0, "stt_i": 0, "vals": [], "defer_gate": None}
        deq_dma_meta = []

        def _deq_srcs(which, kc, j):
            if which == "w1":
                return (q1_d[128 * kc:128 * (kc + 1), 512 * j:512 * (j + 1)],
                        sc1_d[0:1, F * kc + 512 * j: F * kc + 512 * (j + 1)],
                        w1_bf[:, F * kc + 512 * j: F * kc + 512 * (j + 1)])
            return (q2_d[128 * kc:128 * (kc + 1), 512 * j:512 * (j + 1)],
                    sc2_d[0:1, D * kc + 512 * j: D * kc + 512 * (j + 1)],
                    w2_bf[:, D * kc + 512 * j: D * kc + 512 * (j + 1)])

        def _deq_emit_dma():
            n = deq["dma_i"]
            deq["dma_i"] += 1
            which, kc, j = deq_pieces[n]
            par = n % 2
            src_int, src_sc, dst = _deq_srcs(which, kc, j)
            gate = deq["vals"][n - 2] if n >= 2 else 0
            if n == N_NONDEF and deq["defer_gate"] is not None:
                p.wait("gpsimd", "T", deq["defer_gate"])
            if gate:
                p.wait("gpsimd", "V", gate)
            xi = gdma(int_st[par], src_int, key="Xd%d" % par)
            if gate:
                p.wait("sync", "V", gate)
            xs = sdma_load(scbc[par], src_sc.partition_broadcast(128),
                           key="Xs%d" % par)
            deq_dma_meta.append((par, xi, xs, dst, which, kc))

        def _deq_emit_stt():
            n = deq["stt_i"]
            deq["stt_i"] += 1
            par, xi, xs, dst, which, kc = deq_dma_meta[n]
            p.wait("vector", "Xd%d" % par, xi)
            p.wait("vector", "Xs%d" % par, xs)
            v = vop(lambda e, pa=par, d=dst: e.scalar_tensor_tensor(
                out=d, in0=int_st[pa], scalar=-128.0, in1=scbc[pa],
                op0=OP.add, op1=OP.mult))
            deq["vals"].append(v)
            deq_release[(which, kc)] = v

        def deq_step(k, limit=None):
            for _ in range(k):
                lim = len(deq_pieces) if limit is None else limit
                while deq["dma_i"] < min(deq["stt_i"] + 2, lim):
                    _deq_emit_dma()
                if deq["stt_i"] < deq["dma_i"]:
                    _deq_emit_stt()

        xp1_release = [0, 0]
        x1_done = [0] * NT1

        def emit_p1_load(i):
            if i >= NT1:
                return
            if i >= 2:
                p.wait("gpsimd", "T", xp1_release[i % 2])
            x1_done[i] = gdma(
                xp1[i % 2].rearrange("p (c t) -> p c t", c=NCH),
                xt.rearrange("(c p) t -> p c t", p=128)[:, :, P1T * i:P1T * (i + 1)],
                key="Xx")

        emit_p1_load(0)
        emit_p1_load(1)

        p.wait("tensor", "Xw", p.cnt["Xw"])
        p.wait("tensor", "V", v_setup)

        NSUB_TOT = T // 128
        kv_fold = 0
        rel_k = [0, 0]
        rel_kS = 0
        rel_kV = 0
        rel_vV = 0
        rel_pesc = 0
        ktm_ready = [0] * NSUB_TOT

        def kv_bank_col(pr):
            return (4 + pr // 3, 130 * (pr % 3)) if pr < 6 else (6, 130 * (pr - 6))

        for i in range(NT1):
            p.wait("tensor", "Xx", x1_done[i])
            for s in range(NSUB):
                g = i * NSUB + s
                par = g % 2
                p.wait("tensor", "S", rel_kS)
                p.wait("tensor", "V", rel_kV)
                for fh in range(2):
                    mm(P[fh][:, 0:512], sel2k, bias2[:, 512 * fh:512 * (fh + 1)],
                       start=True, stop=False)
                    for kc in range(NCH):
                        lhsT = xp1[i % 2][:, P1T * kc + 128 * s: P1T * kc + 128 * (s + 1)]
                        rhs = w2_bf[:, 16 * D + 1024 * kc + 512 * fh: 16 * D + 1024 * kc + 512 * (fh + 1)]
                        mm(P[fh][:, 0:512], lhsT, rhs, start=False, stop=(kc == NCH - 1),
                           inc=(kc == NCH - 1))
                tk = p.cnt["T"]
                p.wait("tensor", "V", rel_vV)
                for fh in range(2):
                    mm(P[2 + fh][:, 0:512], sel2v, bias2[:, 512 * fh:512 * (fh + 1)],
                       start=True, stop=False)
                    for kc in range(NCH):
                        lhsT = xp1[i % 2][:, P1T * kc + 128 * s: P1T * kc + 128 * (s + 1)]
                        rhs = w2_bf[:, 24 * D + 1024 * kc + 512 * fh: 24 * D + 1024 * kc + 512 * (fh + 1)]
                        mm(P[2 + fh][:, 0:512], lhsT, rhs, start=False, stop=(kc == NCH - 1),
                           inc=(kc == NCH - 1))
                tv = p.cnt["T"]
                if g >= 1:
                    gp = g - 1
                    p.wait("tensor", "V", ktm_ready[gp])
                    p.wait("tensor", "V", kv_fold)
                    for pr in range(8):
                        bank, col = kv_bank_col(pr)
                        mm(P[bank][:, col:col + 130], ktm[gp % 2][:, 128 * pr:128 * (pr + 1)],
                           vtm[gp % 2][:, 130 * pr:130 * (pr + 1)],
                           start=True, stop=True, inc=(pr == 7))
                    t_kvmm = p.cnt["T"]
                    p.wait("vector", "T", t_kvmm)
                    vop(lambda e: e.tensor_tensor(
                        out=kv_acc[:, 0:390], in0=kv_acc[:, 0:390],
                        in1=P[4][:, 0:390], op=OP.add), inc=False)
                    vop(lambda e: e.tensor_tensor(
                        out=kv_acc[:, 390:780], in0=kv_acc[:, 390:780],
                        in1=P[5][:, 0:390], op=OP.add), inc=False)
                    kv_fold = vop(lambda e: e.tensor_tensor(
                        out=kv_acc[:, 780:1040], in0=kv_acc[:, 780:1040],
                        in1=P[6][:, 0:260], op=OP.add))
                p.wait("scalar", "T", tk)
                p.wait("scalar", "V", rel_pesc)
                se = [0, 0]
                sk = [0, 0]
                for fh in range(2):
                    se[fh] = sop(lambda e, f=fh: e.activation(
                        out=p1esc[f], in_=P[f][:, 0:512], func=AF.Exp), inc=False)
                    sk[fh] = sop(lambda e, f=fh, pa=par: e.activation(
                        out=ktm[pa][:, 512 * f:512 * (f + 1)], in_=P[f][:, 0:512],
                        func=AF.Identity, bias=ones_pp[:, 0:1]), drain=(fh == 1))
                rel_kS = sk[1]
                rel_kV = 0
                for fh in range(2):
                    p.wait("vector", "S", sk[fh])
                    vop(lambda e, f=fh: e.tensor_scalar_min(p1esc[f], p1esc[f], 1.0), inc=False)
                    rel_pesc = vop(lambda e, f=fh, pa=par: e.tensor_tensor(
                        out=ktm[pa][:, 512 * f:512 * (f + 1)],
                        in0=ktm[pa][:, 512 * f:512 * (f + 1)],
                        in1=p1esc[f], op=OP.max), inc=(fh == 1))
                p.wait("vector", "T", tv)
                for fh in range(2):
                    last = vop(lambda e, f=fh, pa=par: e.tensor_copy(
                        vtm[pa].rearrange("p (q c) -> p q c", c=130)[:, 4 * f:4 * (f + 1), 0:128],
                        P[2 + f][:, 0:512].rearrange("p (a c) -> p a c", c=128)),
                        inc=(fh == 1))
                rel_vV = last
                ktm_ready[g] = last
                deq_step(3, limit=N_NONDEF)
            xp1_release[i % 2] = p.cnt["T"]
            if i + 2 < NT1:
                emit_p1_load(i + 2)
        gp = NSUB_TOT - 1
        p.wait("tensor", "V", ktm_ready[gp])
        p.wait("tensor", "V", kv_fold)
        for pr in range(8):
            bank, col = kv_bank_col(pr)
            mm(P[bank][:, col:col + 130], ktm[gp % 2][:, 128 * pr:128 * (pr + 1)],
               vtm[gp % 2][:, 130 * pr:130 * (pr + 1)],
               start=True, stop=True, inc=(pr == 7))
        t_kv_done = p.cnt["T"]
        p.wait("vector", "T", t_kv_done)
        vop(lambda e: e.tensor_tensor(out=kv_acc[:, 0:390], in0=kv_acc[:, 0:390],
                                      in1=P[4][:, 0:390], op=OP.add), inc=False)
        vop(lambda e: e.tensor_tensor(out=kv_acc[:, 390:780], in0=kv_acc[:, 390:780],
                                      in1=P[5][:, 0:390], op=OP.add), inc=False)
        kv_fold = vop(lambda e: e.tensor_tensor(out=kv_acc[:, 780:1040],
                                                in0=kv_acc[:, 780:1040],
                                                in1=P[6][:, 0:260], op=OP.add))

        if debug:
            tap("ktm0", ktm[1], "V", ktm_ready[1])
            tap("vtm0", vtm[1], "V", ktm_ready[1])

        for pr in range(8):
            base = 130 * pr
            vop(lambda e, b=base, pr_=pr: e.tensor_copy(
                kv_sb[0:64, HD * pr_:HD * pr_ + HD], kv_acc[0:64, b:b + 64]), inc=False)
            vop(lambda e, b=base, pr_=pr: e.tensor_copy(
                kv_sb[64:128, HD * pr_:HD * pr_ + HD], kv_acc[64:128, b + 64:b + 128]), inc=False)
        vop(lambda e: e.memset(ksum_bd[:], 0), inc=False)
        for c in range(NCH):
            base = 130 * c
            vop(lambda e, b=base, c_=c: e.tensor_copy(
                ksum_bd[0:64, 18 * c_:18 * c_ + 1], kv_acc[0:64, b + 128:b + 129]), inc=False)
            vop(lambda e, b=base, c_=c: e.tensor_copy(
                ksum_bd[64:128, 18 * c_ + 1:18 * c_ + 2], kv_acc[64:128, b + 128:b + 129]))
        v_kvready = p.cnt["V"]
        for b in range(8):
            set_release(b, ("V", v_kvready))
        tap("kv", kv_sb[:], "V", v_kvready)
        tap("ksbd", ksum_bd[:], "V", v_kvready)

        x2_done = [0] * NT2
        xbf_release = {"v": 0}

        def emit_p2_load(i):
            if i >= NT2:
                return
            p.wait("gpsimd", "V", xbf_release["v"])
            if i == 0:
                p.wait("gpsimd", "T", t_kv_done)
            x2_done[i] = gdma(
                xbf.rearrange("p (c t) -> p c t", c=NCH),
                xt.rearrange("(c p) t -> p c t", p=128)[:, :, TT * i:TT * (i + 1)],
                key="Xx")

        emit_p2_load(0)

        rot01 = {"idx": 0}
        esc_rel = [0, 0]

        ts = [dict() for _ in range(NT2)]

        def stage_qproj(t):
            p.wait("tensor", "Xx", p.cnt["Xx"])
            ev_s = [0] * NCH
            for fc in range(NCH):
                bank = rot01["idx"]
                rot01["idx"] ^= 1
                twait_bank(bank)
                for kc in range(NCH):
                    mm(P[bank][:, 0:256], wq_bf[:, D * kc + 128 * fc: D * kc + 128 * (fc + 1)],
                       xbf[:, TT * kc:TT * (kc + 1)], start=(kc == 0), stop=(kc == NCH - 1),
                       inc=(kc == NCH - 1))
                tq = p.cnt["T"]
                p.wait("scalar", "T", tq)
                sop(lambda e, b=bank, f=fc: e.activation(
                    out=sqb[:, TT * f:TT * (f + 1)], in_=P[b][:, 0:256], func=AF.Exp,
                    bias=bq_pp[:, f:f + 1]), inc=False)
                ev_s[fc] = sop(lambda e, b=bank, f=fc: e.activation(
                    out=qbf[:, TT * f:TT * (f + 1)], in_=P[b][:, 0:256], func=AF.Identity,
                    bias=bq1_pp[:, f:f + 1]), drain=True)
                set_release(bank, ("S", ev_s[fc]))
            p.wait("vector", "S", ev_s[NCH - 1])
            vop(lambda e: e.tensor_scalar_min(sqb, sqb, 1.0), inc=False)
            v_q2 = vop(lambda e: e.tensor_tensor(out=qbf, in0=qbf, in1=sqb, op=OP.max))
            ts[t]["q2"] = v_q2
            if debug and t == 0:
                tap("q2", qbf, "V", v_q2)

        def stage_norm(t):
            twait_bank(6)
            p.wait("tensor", "V", ts[t]["q2"])
            for kc in range(NCH):
                mm(P[6][0:16, 0:256], ksum_bd[:, 16 * kc:16 * (kc + 1)],
                   qbf[:, TT * kc:TT * (kc + 1)], start=(kc == 0), stop=(kc == NCH - 1),
                   inc=(kc == NCH - 1))
            t_norm = p.cnt["T"]
            p.wait("vector", "T", t_norm)
            vop(lambda e: e.tensor_scalar_max(P[6][0:16, 256:512], P[6][0:16, 0:256],
                                              EPS_ATTN), inc=False)
            vop(lambda e: e.reciprocal(out=P[6][0:16, 256:512],
                                       in_=P[6][0:16, 256:512]), inc=False)
            v_recip = vop(lambda e: e.tensor_copy(nrow_bf, P[6][0:16, 256:512]))
            ts[t]["recip"] = v_recip
            set_release(6, ("V", v_recip))
            if debug and t == 0:
                tap("nrow", nrow_bf, "V", v_recip)

        def stage_bcast_recip(t):
            v_q3 = [0] * NCH
            for c in range(NCH):
                bank = 6 + c % 2
                twait_bank(bank)
                if c < 2:
                    p.wait("tensor", "V", ts[t]["recip"])
                mm(P[bank][:, 0:256], selb[:, 128 * c:128 * (c + 1)], nrow_bf,
                   start=True, stop=True, inc=True)
                tb = p.cnt["T"]
                p.wait("vector", "T", tb)
                v_q3[c] = vop(lambda e, b=bank, c_=c: e.tensor_tensor(
                    out=qbf[:, TT * c_:TT * (c_ + 1)], in0=qbf[:, TT * c_:TT * (c_ + 1)],
                    in1=P[b][:, 0:256], op=OP.mult))
                set_release(bank, ("V", v_q3[c]))
            ts[t]["q3"] = v_q3
            if debug and t == 0:
                tap("q3", qbf, "V", v_q3[NCH - 1])

        def stage_qkv(t):
            ev_mg = [0] * NCH
            for c in range(NCH):
                bank = 6 + c % 2
                twait_bank(bank)
                p.wait("tensor", "V", ts[t]["q3"][c])
                mm(P[bank][0:64, 0:256], kv_sb[0:64, HD * c:HD * (c + 1)],
                   qbf[0:64, TT * c:TT * (c + 1)], start=True, stop=True)
                mm(P[bank][64:128, 0:256], kv_sb[64:128, HD * c:HD * (c + 1)],
                   qbf[64:128, TT * c:TT * (c + 1)], start=True, stop=True,
                   inc=True, tp=(64, 64))
                tm = p.cnt["T"]
                p.wait("scalar", "T", tm)
                ev_mg[c] = sop(lambda e, b=bank, c_=c: e.activation(
                    out=mgbf[:, TT * c_:TT * (c_ + 1)], in_=P[b][:, 0:256], func=AF.Copy),
                    drain=True)
                set_release(bank, ("S", ev_mg[c]))
            ts[t]["mg"] = ev_mg[NCH - 1]
            if debug and t == 0:
                tap("mg", mgbf, "S", ev_mg[NCH - 1])

        def stage_attnproj(t):
            ev_r1 = [0] * NCH
            p.wait("tensor", "S", ts[t]["mg"])
            for fc in range(NCH):
                bank = rot01["idx"]
                rot01["idx"] ^= 1
                twait_bank(bank)
                for kc in range(NCH):
                    mm(P[bank][:, 0:256], wo_bf[:, D * kc + 128 * fc: D * kc + 128 * (fc + 1)],
                       mgbf[:, TT * kc:TT * (kc + 1)], start=(kc == 0), stop=(kc == NCH - 1),
                       inc=(kc == NCH - 1))
                ta = p.cnt["T"]
                p.wait("vector", "T", ta)
                ev_r1[fc] = vop(lambda e, b=bank, f=fc: e.scalar_tensor_tensor(
                    out=r1bf[:, TT * f:TT * (f + 1)], in0=P[b][:, 0:256],
                    scalar=bo_pp[:, f:f + 1], in1=xbf[:, TT * f:TT * (f + 1)],
                    op0=OP.add, op1=OP.add))
                set_release(bank, ("V", ev_r1[fc]))
            ts[t]["r1"] = ev_r1[NCH - 1]
            xbf_release["v"] = ev_r1[NCH - 1]
            if debug and t == 0:
                tap("r1", r1bf, "V", ev_r1[NCH - 1])
            emit_p2_load(t + 1)

        def ln_rows(src_bf, scale_m, scale_sq, v_dep):
            v_sq = vop(lambda e: e.tensor_tensor(out=sqb, in0=src_bf, in1=src_bf, op=OP.mult))
            twait_bank(6)
            twait_bank(7)
            p.wait("tensor", "V", v_sq)
            for pp in range(4):
                mm(P[6][0:1, 0:512], onec_bf[:], src_bf[:, 512 * pp:512 * (pp + 1)],
                   start=(pp == 0), stop=(pp == 3))
            for pp in range(4):
                mm(P[7][0:1, 0:512], onec_bf[:], sqb[:, 512 * pp:512 * (pp + 1)],
                   start=(pp == 0), stop=(pp == 3), inc=(pp == 3))
            t_st = p.cnt["T"]
            p.wait("vector", "T", t_st)
            vop(lambda e: e.tensor_scalar_mul(row_b, P[6][0:1, 0:256], scale_m), inc=False)
            vop(lambda e: e.scalar_tensor_tensor(
                out=row_b, in0=P[6][0:1, 256:512], scalar=scale_m, in1=row_b,
                op0=OP.mult, op1=OP.add), inc=False)
            vop(lambda e: e.tensor_scalar(out=row_c, in0=P[7][0:1, 0:256],
                                          scalar1=scale_sq, scalar2=EPS_LN,
                                          op0=OP.mult, op1=OP.add), inc=False)
            vop(lambda e: e.scalar_tensor_tensor(
                out=row_c, in0=P[7][0:1, 256:512], scalar=scale_sq, in1=row_c,
                op0=OP.mult, op1=OP.add), inc=False)
            vop(lambda e: e.tensor_tensor(out=P[6][0:1, 0:256], in0=row_b, in1=row_b,
                                          op=OP.mult), inc=False)
            v_var = vop(lambda e: e.tensor_tensor(out=row_c, in0=row_c,
                                                  in1=P[6][0:1, 0:256], op=OP.subtract))
            p.wait("scalar", "V", v_var)
            s_sqrt = sop(lambda e: e.activation(out=row_c, in_=row_c, func=AF.Sqrt))
            p.wait("vector", "S", s_sqrt)
            v_rstd = vop(lambda e: e.reciprocal(out=row_c, in_=row_c))
            set_release(6, ("V", v_rstd))
            return v_rstd

        def ln_bcast_norm(src_bf, dst_bf, a_scale, v_rows):
            va = v_rows
            if a_scale != 1.0:
                va = vop(lambda e: e.tensor_scalar_mul(row_c, row_c, a_scale))
            vb = vop(lambda e: e.scalar_tensor_tensor(
                out=row_b, in0=row_b, scalar=-1.0, in1=row_c, op0=OP.mult, op1=OP.mult))
            p.wait("tensor", "V", max(va, vb))
            mm(P[7][:, 0:256], oner_f, row_c, start=True, stop=True)
            mm(P[7][:, 256:512], oner_f, row_b, start=True, stop=True, inc=True)
            t_bc = p.cnt["T"]
            p.wait("vector", "T", t_bc)
            a_tile = P[7][:, 0:256]
            b_tile = P[7][:, 256:512]
            a_rep = bass.AP(a_tile.tensor, a_tile.offset, [a_tile.ap[0], [0, NCH], a_tile.ap[1]])
            b_rep = bass.AP(b_tile.tensor, b_tile.offset, [b_tile.ap[0], [0, NCH], b_tile.ap[1]])
            src3 = src_bf.rearrange("p (c t) -> p c t", c=NCH)
            dst3 = dst_bf.rearrange("p (c t) -> p c t", c=NCH)
            vop(lambda e: e.tensor_tensor(out=dst3, in0=src3, in1=a_rep, op=OP.mult), inc=False)
            v_done = vop(lambda e: e.tensor_tensor(out=dst3, in0=dst3, in1=b_rep, op=OP.add))
            set_release(7, ("V", v_done))
            return v_done

        def stage_ln1_rows(t):
            p.wait("vector", "V", ts[t]["r1"])
            ts[t]["ln1rows"] = ln_rows(r1bf, 1.0 / D, 1.0 / D, ts[t]["r1"])

        def stage_ln1_bcast(t):
            v_x2 = ln_bcast_norm(r1bf, x2s[t % 2], 2.0, ts[t]["ln1rows"])
            ts[t]["x2"] = v_x2
            if debug and t == 0:
                tap("x2", x2s[t % 2], "V", v_x2)

        def stage_ln2_rows(t):
            ts[t]["ln2rows"] = ln_rows(x2s[t % 2], 1.0 / (2 * D), 1.0 / (4 * D),
                                       ts[t]["x2"])

        def stage_ln2_bcast(t):
            v_hraw = ln_bcast_norm(x2s[t % 2], hbf, 0.5, ts[t]["ln2rows"])
            v_h = 0
            for c in range(NCH):
                v_h = vop(lambda e, c_=c: e.tensor_scalar(
                    out=hbf[:, TT * c_:TT * (c_ + 1)], in0=hbf[:, TT * c_:TT * (c_ + 1)],
                    scalar1=gam_pp[:, c_:c_ + 1], scalar2=bet_pp[:, c_:c_ + 1],
                    op0=OP.mult, op1=OP.add))
            ts[t]["h"] = v_h
            if debug and t == 0:
                tap("h", hbf, "V", v_h)

        def stage_fc1(f, fc0, fc1_):
            if f == 0 and fc0 == 0:
                p.wait("tensor", "V", max(deq_release[("w1", kc)] for kc in range(NCH)))
            ev_y = ts[f].setdefault("y", {})
            for fc in range(fc0, fc1_):
                j = fc % 16
                bank = rot01["idx"]
                rot01["idx"] ^= 1
                twait_bank(bank)
                if fc == 0:
                    p.wait("tensor", "V", ts[f]["h"])
                for kc in range(NCH):
                    mm(P[bank][:, 0:256],
                       w1_bf[:, F * kc + 128 * fc: F * kc + 128 * (fc + 1)],
                       hbf[:, TT * kc:TT * (kc + 1)], start=(kc == 0), stop=(kc == NCH - 1),
                       inc=(kc == NCH - 1))
                tf = p.cnt["T"]
                p.wait("scalar", "T", tf)
                if fc == fc0:
                    p.wait("scalar", "ST", p.cnt["ST"])
                ev_y[fc] = sop(lambda e, b=bank, fcx=fc, j_=j: e.activation(
                    out=ybf[:, TT * j_:TT * (j_ + 1)], in_=P[b][:, 0:256], func=AF.Silu,
                    bias=fc1b_pp[:, fcx:fcx + 1]), drain=True)
                set_release(bank, ("S", ev_y[fc]))
            if debug and f == 0 and fc1_ == 16:
                tap("y0", ybf, "S", ev_y[15])

        def stage_fc2(f, half):
            ev_y = ts[f]["y"]
            for zbank in range(2, 6):
                twait_bank(zbank)
                for sub in range(2):
                    fco = 2 * (zbank - 2) + sub
                    zcol = 256 * sub
                    for j in range(16):
                        kc = 16 * half + j
                        if fco == 0:
                            p.wait("tensor", "S", ev_y[kc])
                        if f == 0 and zbank == 2 and sub == 0 and ("w2", kc) in deq_release:
                            p.wait("tensor", "V", deq_release[("w2", kc)])
                        mm(P[zbank][:, zcol:zcol + 256],
                           w2_bf[:, D * kc + 128 * fco: D * kc + 128 * (fco + 1)],
                           ybf[:, TT * j:TT * (j + 1)],
                           start=(j == 0), stop=(j == 15),
                           inc=(j == 15))
                t_z = p.cnt["T"]
                if half == 0:
                    p.wait("scalar", "T", t_z)
                    if zbank == 2 and f + 1 < NT2 and "x2" in ts[f + 1]:
                        p.wait("scalar", "V", ts[f + 1]["x2"])
                    sz = 0
                    for sub in range(2):
                        fco = 2 * (zbank - 2) + sub
                        sz = sop(lambda e, b=zbank, c=256 * sub, f_=fco: e.activation(
                            out=zacc[:, TT * f_:TT * (f_ + 1)], in_=P[b][:, c:c + 256],
                            func=AF.Copy))
                    bank_release[zbank] = [("S", sz)]
                else:
                    p.wait("scalar", "T", t_z)
                    x2w = x2s[f % 2]
                    ev_w = 0
                    s_last = 0
                    for sub in range(2):
                        fco = 2 * (zbank - 2) + sub
                        slot = fco % len(esc)
                        if esc_rel[slot]:
                            p.wait("scalar", "V", esc_rel[slot])
                        s_last = sop(lambda e, b=zbank, c=256 * sub, f_=fco: e.activation(
                            out=esc[f_ % len(esc)], in_=P[b][:, c:c + 256],
                            func=AF.Identity, bias=fc2bh_pp[:, f_:f_ + 1], scale=0.5))
                        p.wait("vector", "S", s_last)
                        vop(lambda e, f_=fco: e.scalar_tensor_tensor(
                            out=esc[f_ % len(esc)], in0=zacc[:, TT * f_:TT * (f_ + 1)],
                            scalar=0.5, in1=esc[f_ % len(esc)],
                            op0=OP.mult, op1=OP.add), inc=False)
                        ev_w = vop(lambda e, f_=fco: e.tensor_tensor(
                            out=x2w[:, TT * f_:TT * (f_ + 1)], in0=esc[f_ % len(esc)],
                            in1=x2w[:, TT * f_:TT * (f_ + 1)], op=OP.add))
                        esc_rel[slot] = ev_w
                    bank_release[zbank] = [("S", s_last)]
                    ts[f]["w"] = ev_w
            ts[f]["fc2_t"] = p.cnt["T"]

        def stage_ln3_rows(f):
            x2w = x2s[f % 2]
            v_rows = ln_rows(x2w, 1.0 / D, 1.0 / D, ts[f]["w"])
            vb = vop(lambda e: e.scalar_tensor_tensor(
                out=row_b, in0=row_b, scalar=-1.0, in1=row_c, op0=OP.mult, op1=OP.mult))
            ts[f]["ln3rows"] = vb

        def stage_ln3_store(f):
            x2w = x2s[f % 2]
            twait_bank(7)
            p.wait("tensor", "V", ts[f]["ln3rows"])
            mm(P[7][:, 0:256], oner_f, row_c, start=True, stop=True)
            mm(P[7][:, 256:512], oner_f, row_b, start=True, stop=True, inc=True)
            t_bc3 = p.cnt["T"]
            p.wait("vector", "T", t_bc3)
            a_tile = P[7][:, 0:256]
            b_tile = P[7][:, 256:512]
            a_rep = bass.AP(a_tile.tensor, a_tile.offset, [a_tile.ap[0], [0, NCH], a_tile.ap[1]])
            b_rep = bass.AP(b_tile.tensor, b_tile.offset, [b_tile.ap[0], [0, NCH], b_tile.ap[1]])
            src3 = x2w.rearrange("p (c t) -> p c t", c=NCH)
            dst3 = outfull.rearrange("p (c t) -> p c t", c=NCH)
            vop(lambda e: e.tensor_tensor(out=dst3, in0=src3, in1=a_rep, op=OP.mult), inc=False)
            vlast = vop(lambda e: e.tensor_tensor(out=dst3, in0=dst3, in1=b_rep, op=OP.add))
            p.wait("sync", "V", vlast)
            sdma_store(
                out_t.rearrange("(c p) t -> p c t", p=128)[:, :, TT * f:TT * (f + 1)],
                dst3)
            set_release(7, ("V", vlast))

        if p1_only:
            if debug and dbg.get("_qs17"):
                pass
            p.wait("sync", "ST", p.cnt["ST"])
            with nc.Block() as block:
                def runner(events):
                    def _(eng):
                        for item in events:
                            if item[0] == "do":
                                item[1](eng)
                            else:
                                _, key, val = item
                                eng.wait_ge(SEMS[key], val)
                    return _
                block.tensor(runner(p.ev["tensor"]))
                block.vector(runner(p.ev["vector"]))
                block.scalar(runner(p.ev["scalar"]))
                block.sync(runner(p.ev["sync"]))
                block.gpsimd(runner(p.ev["gpsimd"]))
            return nc

        p.wait("tensor", "Xq", p.cnt["Xq"])
        if prol >= 1:
            stage_qproj(0)
        if prol >= 2:
            stage_norm(0)
        if prol >= 3:
            stage_bcast_recip(0)
        if prol >= 4:
            stage_qkv(0)

        deq["defer_gate"] = t_kv_done
        deq_step(6)

        n_iters = NT2 + 1 if max_iters is None else max_iters
        for it in range(n_iters):
            if it <= 2:
                deq_step(13)
            if it >= 2:
                stage_ln3_store(it - 2)
            if it < NT2:
                stage_attnproj(it)
            if it >= 1:
                stage_fc1(it - 1, 0, 8)
            if it < NT2:
                stage_ln1_rows(it)
            if it >= 1:
                stage_fc1(it - 1, 8, 16)
            if it < NT2:
                stage_ln1_bcast(it)
            if it + 1 < NT2:
                stage_qproj(it + 1)
            if it >= 1:
                stage_fc2(it - 1, 0)
            if it + 1 < NT2:
                stage_norm(it + 1)
            if it >= 1:
                stage_fc1(it - 1, 16, 24)
            if it + 1 < NT2:
                stage_bcast_recip(it + 1)
            if it >= 1:
                stage_fc1(it - 1, 24, 32)
            if it == 0:
                deq_step(len(deq_pieces))
            if it < NT2:
                stage_ln2_rows(it)
                stage_ln2_bcast(it)
            if it >= 1:
                stage_fc2(it - 1, 1)
            if it + 1 < NT2:
                stage_qkv(it + 1)
            if it >= 1:
                stage_ln3_rows(it - 1)

        if NT2 >= 1 and (max_iters is None or max_iters > NT2):
            stage_ln3_store(NT2 - 1)
        p.wait("sync", "ST", p.cnt["ST"])

        with nc.Block() as block:
            def runner(events):
                def _(eng):
                    for item in events:
                        if item[0] == "do":
                            item[1](eng)
                        else:
                            _, key, val = item
                            eng.wait_ge(SEMS[key], val)
                return _
            block.tensor(runner(p.ev["tensor"]))
            block.vector(runner(p.ev["vector"]))
            block.scalar(runner(p.ev["scalar"]))
            block.sync(runner(p.ev["sync"]))
            block.gpsimd(runner(p.ev["gpsimd"]))

    return nc


_NC_CACHE = None


def prep_in_maps(L0_attn_x, L0_attn_Wq, L0_attn_bq, L0_attn_Wk, L0_attn_bk,
                 L0_attn_Wv, L0_attn_bv, L0_attn_Wo, L0_attn_bo,
                 L0_ffn_ln_gamma, L0_ffn_ln_beta, L0_ffn_fc1_q, L0_ffn_fc1_scales,
                 L0_ffn_fc1_bias, L0_ffn_fc2_q, L0_ffn_fc2_scales, L0_ffn_fc2_bias):
    asc = np.ascontiguousarray
    common = {
        "wq_t": asc(np.asarray(L0_attn_Wq, np.float32).T),
        "wk_t": asc(np.asarray(L0_attn_Wk, np.float32).T),
        "wv_t": asc(np.asarray(L0_attn_Wv, np.float32).T),
        "wo_t": asc(np.asarray(L0_attn_Wo, np.float32).T),
        "bq_d": asc(np.asarray(L0_attn_bq, np.float32).reshape(NCH, 128).T),
        "bo_d": asc(np.asarray(L0_attn_bo, np.float32).reshape(NCH, 128).T),
        "bias2_d": asc(np.stack([np.asarray(L0_attn_bk, np.float32),
                                 np.asarray(L0_attn_bv, np.float32)], axis=0)),
        "gam_d": asc(np.asarray(L0_ffn_ln_gamma, np.float32).reshape(NCH, 128).T),
        "bet_d": asc(np.asarray(L0_ffn_ln_beta, np.float32).reshape(NCH, 128).T),
        "fc1b_d": asc(np.asarray(L0_ffn_fc1_bias, np.float32).reshape(FCH, 128).T),
        "fc2b_d": asc(np.asarray(L0_ffn_fc2_bias, np.float32).reshape(NCH, 128).T),
        "q1_d": asc(np.asarray(L0_ffn_fc1_q, np.int32)),
        "q2_d": asc(np.asarray(L0_ffn_fc2_q, np.int32)),
        "sc1_d": asc(np.asarray(L0_ffn_fc1_scales, np.float32).reshape(1, NCH * F)),
        "sc2_d": asc(np.asarray(L0_ffn_fc2_scales, np.float32).reshape(1, FCH * D)),
    }
    sel = np.zeros((16, D), np.float32)
    for c in range(NCH):
        sel[2 * c, 128 * c:128 * c + 64] = 1.0
        sel[2 * c + 1, 128 * c + 64:128 * c + 128] = 1.0
    common["sel_d"] = sel
    sel2 = np.zeros((2, 256), np.float32)
    sel2[0, 0:128] = 1.0
    sel2[1, 128:256] = 1.0
    common["sel2_d"] = sel2
    common["onec_d"] = np.ones((128, 1), np.float32)
    common["oner_d"] = np.ones((1, 128), np.float32)

    x = np.asarray(L0_attn_x, np.float32)
    in_maps = []
    for b in range(8):
        m = dict(common)
        m["xt"] = asc(x[b].T)
        in_maps.append(m)
    return in_maps


def kernel(**inputs):
    global _NC_CACHE
    if _NC_CACHE is None:
        _NC_CACHE = build_nc()
    nc = _NC_CACHE
    in_maps = prep_in_maps(**inputs)
    res = run_bass_kernel_spmd(nc, in_maps, core_ids=list(range(8)))
    out = np.stack([np.ascontiguousarray(r["out_t"].T) for r in res.results], axis=0)
    return out
